# version 12
# baseline (speedup 1.0000x reference)
import numpy as np
import ml_dtypes
from contextlib import ExitStack

import concourse.bass as bass
import concourse.mybir as mybir
from concourse.bass_utils import run_bass_kernel_spmd

F32 = mybir.dt.float32
BF16 = mybir.dt.bfloat16
AF = mybir.ActivationFunctionType
ALU = mybir.AluOpType
AX = mybir.AxisListType.X

NCORES = 8
B, S, D = 4, 4096, 1024
NTOK = B * S
OWN = NTOK // NCORES
H, DK = 16, 64
E, DH = 8, 2048
NCH = D // 128
SEQS = B

BIGNEG = -1.0e9


ENGINES = ("tensor", "vector", "scalar", "gpsimd", "sync")


class Sched:
    def __init__(self):
        self.events = {e: [] for e in ENGINES}
        self.count = {e: 0 for e in ENGINES}

    def emit(self, eng, fn, waits=(), inc=0):
        wd = {}
        for t in waits:
            if t is None:
                continue
            we, wv = t
            if wv > wd.get(we, 0):
                wd[we] = wv
        if inc:
            self.count[eng] += inc
        tok = (eng, self.count[eng]) if inc else None
        self.events[eng].append((wd, fn, inc))
        return tok


class Buf:

    __slots__ = ("w", "rs")

    def __init__(self):
        self.w = None
        self.rs = []

    def rdeps(self):
        return [self.w] if self.w else []

    def wdeps(self):
        return ([self.w] if self.w else []) + self.rs


class Builder:
    def __init__(self, nc):
        self.nc = nc
        self.S = Sched()
        self.fence = []
        self.window = 0
        self.hist = []

    def new_phase(self):
        self.fence = [(e, self.S.count[e]) for e in ENGINES
                      if self.S.count[e] > 0]

    def buf(self):
        b = Buf()
        b.rs = list(self.fence)
        return b

    def _hs_fence(self):
        snap = [(e, self.S.count[e]) for e in ENGINES if self.S.count[e] > 0]
        if not self.window:
            return snap
        self.hist.append(snap)
        idx = len(self.hist) - 1 - self.window
        return self.hist[idx] if idx >= 0 else []

    def op(self, eng, fn, reads=(), writes=(), extra=(), inc=1):
        waits = []
        if getattr(self, "hard_sync", False):
            waits += self._hs_fence()
        for b in reads:
            waits += b.rdeps()
        for b in writes:
            waits += b.wdeps()
        waits += [t for t in extra if t is not None]
        tok = self.S.emit(eng, fn, waits, inc)
        for b in reads:
            b.rs.append(tok)
        for b in writes:
            b.w = tok
            b.rs = []
        return tok

    def group(self, eng, fns, reads=(), writes=(), extra=(), inc=1):
        waits = []
        if getattr(self, "hard_sync", False):
            waits += self._hs_fence()
        for b in reads:
            waits += b.rdeps()
        for b in writes:
            waits += b.wdeps()
        waits += [t for t in extra if t is not None]
        tok = None
        for i, fn in enumerate(fns):
            last = i == len(fns) - 1
            tok2 = self.S.emit(eng, fn, waits if i == 0 else (), inc if last else 0)
            if last:
                tok = tok2
        for b in reads:
            b.rs.append(tok)
        for b in writes:
            b.w = tok
            b.rs = []
        return tok


class Tile:

    def __init__(self, t, off, n):
        self.t = t
        self.off = off
        self.n = n

    def __getitem__(self, idx):
        p, f = idx
        start = self.off + (f.start or 0)
        stop = self.off + (f.stop if f.stop is not None else self.n)
        if f.step in (None, 1):
            return self.t[p, start:stop]
        return self.t[p, start:stop:f.step]


class PTile:

    def __init__(self, t, off, n):
        self.t = t
        self.off = off
        self.n = n

    def __getitem__(self, idx):
        p, f = idx
        start = self.off + (f.start or 0)
        stop = self.off + (f.stop if f.stop is not None else self.n)
        return self.t[p, start:stop]


class Arena:
    def __init__(self, nc, name, n_elems, dtype):
        self._guard = nc.sbuf_tensor(name, [128, n_elems], dtype)
        self.t = None
        self.n = n_elems
        self.mark = 0

    def enter(self, ctx):
        self.t = ctx.enter_context(self._guard)

    def reset(self, to=0):
        self.mark = to

    def alloc(self, n):
        off = (self.mark + 15) // 16 * 16
        self.mark = off + n
        assert self.mark <= self.n, f"arena overflow: {self.mark} > {self.n}"
        return Tile(self.t, off, n)


def build_program():
    nc = bass.Bass("TRN2", target_bir_lowering=False)

    x_own = nc.dram_tensor("x_own", [OWN, D], F32, kind="ExternalInput").ap()
    wq_p = nc.dram_tensor("wq_p", [128, NCH * 128], F32, kind="ExternalInput").ap()
    wk_p = nc.dram_tensor("wk_p", [128, NCH * 128], F32, kind="ExternalInput").ap()
    wv_p = nc.dram_tensor("wv_p", [128, NCH * 128], F32, kind="ExternalInput").ap()
    wo_l = nc.dram_tensor("wo_l", [128, NCORES * 1024], F32, kind="ExternalInput").ap()
    wg_l = nc.dram_tensor("wg_l", [128, NCH * E], F32, kind="ExternalInput").ap()
    bg_bc_in = nc.dram_tensor("bg_bc", [128, E], F32, kind="ExternalInput").ap()
    w1_l = nc.dram_tensor("w1_l", [E, 128, 16 * 8 * 128], F32, kind="ExternalInput").ap()
    b1_l = nc.dram_tensor("b1_l", [128, E * 16 * 4], F32, kind="ExternalInput").ap()
    w2_l = nc.dram_tensor("w2_l", [E, 128, 16 * 1024], F32, kind="ExternalInput").ap()
    b2_l = nc.dram_tensor("b2_l", [128, D], F32, kind="ExternalInput").ap()
    cos_in = nc.dram_tensor("cos_t", [128, S], F32, kind="ExternalInput").ap()
    sin_in = nc.dram_tensor("sin_t", [128, S], F32, kind="ExternalInput").ap()
    masks_in = nc.dram_tensor("masks", [128, 4 * 1024], F32, kind="ExternalInput").ap()
    ident_in = nc.dram_tensor("ident", [128, 128], F32, kind="ExternalInput").ap()
    oneshd_in = nc.dram_tensor("oneshd", [128, 2], F32, kind="ExternalInput").ap()
    onesrow_in = nc.dram_tensor("onesrow", [128, 128], F32, kind="ExternalInput").ap()

    y_own = nc.dram_tensor("y_own", [OWN, D], F32, kind="ExternalOutput").ap()

    xn_ag_in = nc.dram_tensor("xn_ag_in", [OWN, D], BF16, kind="Internal")
    xn_all = nc.dram_tensor("xn_all", [NTOK, D], BF16, kind="Internal")
    ctx_ag_in = nc.dram_tensor("ctx_ag_in", [NTOK, 128], BF16, kind="Internal")
    ctx_all = nc.dram_tensor("ctx_all", [NCORES * NTOK, 128], BF16, kind="Internal")
    ctx_mine = nc.dram_tensor("ctx_mine", [NCORES * OWN, 128], BF16, kind="Internal")

    with ExitStack() as ctx:
        bld = Builder(nc)
        op, group = bld.op, bld.group

        def sbuf(name, shape, dtype):
            return ctx.enter_context(nc.sbuf_tensor(name, shape, dtype))

        acc = sbuf("acc", [128, 16 * 1024], F32)
        ident = sbuf("ident_s", [128, 128], F32)
        oneshd = sbuf("oneshd_s", [128, 2], BF16)
        onesrow = sbuf("onesrow_s", [128, 128], F32)
        negc_bc = sbuf("negc_bc", [128, 1], F32)
        cstat = sbuf("cstat", [1, 16], F32)
        maxes = sbuf("maxes", [1, 4], F32)

        arena_b = Arena(nc, "arena_b", 60544, BF16)
        arena_f = Arena(nc, "arena_f", 6144, F32)
        arena_b.enter(ctx)
        arena_f.enter(ctx)

        pp01 = ctx.enter_context(nc.psum_tensor("pp01", [128, 1024], F32))
        pp23 = ctx.enter_context(nc.psum_tensor("pp23", [128, 1024], F32))
        ps_s = [ctx.enter_context(nc.psum_tensor(f"ps{i}", [128, 512], F32))
                for i in range(4, 8)]
        ps = [PTile(pp01, 0, 512), PTile(pp01, 512, 512),
              PTile(pp23, 0, 512), PTile(pp23, 512, 512),
              PTile(ps_s[0], 0, 512), PTile(ps_s[1], 0, 512),
              PTile(ps_s[2], 0, 512), PTile(ps_s[3], 0, 512)]
        psb = [Buf() for _ in range(8)]
        pp = [pp01, pp01, pp23, pp23]

        sem = {e: ctx.enter_context(nc.semaphore("sem_" + e)) for e in ENGINES}

        d_xnagin = Buf()
        d_xnall = Buf()
        d_ctxagin = Buf()
        d_ctxall = Buf()
        d_ctxmine = Buf()
        d_yout = Buf()

        b_ident = Buf()
        b_oneshd = Buf()
        b_onesrow = Buf()
        op("sync", lambda e: e.dma_start(out=ident[:, :], in_=ident_in),
           writes=[b_ident], inc=16)
        op("gpsimd", lambda e: e.dma_start(out=oneshd[:, :], in_=oneshd_in),
           writes=[b_oneshd], inc=16)
        op("sync", lambda e: e.dma_start(out=onesrow[:, :], in_=onesrow_in),
           writes=[b_onesrow], inc=16)



        arena_b.reset()
        arena_f.reset()
        p1_xn = [arena_b.alloc(1024) for _ in range(2)]
        p1_x = [arena_f.alloc(1024) for _ in range(2)]
        p1_xc = [arena_f.alloc(1024) for _ in range(2)]
        _p1sq = arena_f.alloc(1024)
        p1_sq = [_p1sq, _p1sq]
        p1_st = arena_f.alloc(5 * 16 * 4)
        b_xn = [Buf(), Buf()]
        b_x = [Buf(), Buf()]
        b_xc = [Buf(), Buf()]
        _bsq = Buf()
        b_sq = [_bsq, _bsq]
        b_st = Buf()

        for i in range(16):
            sl = i % 2
            xt, xct, sqt, xnt = p1_x[sl], p1_xc[sl], p1_sq[sl], p1_xn[sl]
            bx, bxc, bsq, bxn = b_x[sl], b_xc[sl], b_sq[sl], b_xn[sl]
            mcol = p1_st[:, 4 * (5 * i + 0): 4 * (5 * i + 0) + 1]
            vcol = p1_st[:, 4 * (5 * i + 1): 4 * (5 * i + 1) + 1]
            sdcol = p1_st[:, 4 * (5 * i + 2): 4 * (5 * i + 2) + 1]
            rcol = p1_st[:, 4 * (5 * i + 3): 4 * (5 * i + 3) + 1]
            scol = p1_st[:, 4 * (5 * i + 4): 4 * (5 * i + 4) + 1]

            op("sync", lambda e, xt=xt, i=i: e.dma_start(
                out=xt[:, 0:1024], in_=x_own[128 * i:128 * (i + 1), :]),
                writes=[bx], inc=16)
            op("vector", lambda e, xt=xt, scol=scol: e.reduce_sum(
                scol, xt[:, 0:1024], axis=AX), reads=[bx], writes=[b_st])
            op("vector", lambda e, mcol=mcol, scol=scol: e.tensor_scalar(
                mcol, scol, 1.0 / D, None, ALU.mult), reads=[b_st], writes=[b_st])
            op("vector", lambda e, xct=xct, xt=xt, mcol=mcol: e.tensor_scalar(
                xct[:, 0:1024], xt[:, 0:1024], mcol, None, ALU.subtract),
                reads=[bx, b_st], writes=[bxc])
            op("vector", lambda e, sqt=sqt, xct=xct: e.tensor_mul(
                sqt[:, 0:1024], xct[:, 0:1024], xct[:, 0:1024]),
                reads=[bxc], writes=[bsq])
            op("vector", lambda e, vcol=vcol, sqt=sqt: e.reduce_sum(
                vcol, sqt[:, 0:1024], axis=AX), reads=[bsq], writes=[b_st])
            op("vector", lambda e, vcol=vcol: e.tensor_scalar(
                vcol, vcol, 1.0 / D, 1e-5, ALU.mult, op1=ALU.add),
                reads=[b_st], writes=[b_st])
            op("scalar", lambda e, sdcol=sdcol, vcol=vcol: e.activation(
                out=sdcol, in_=vcol, func=AF.Sqrt, bias=0.0, scale=1.0),
                reads=[b_st], writes=[b_st])
            op("vector", lambda e, rcol=rcol, sdcol=sdcol: e.reciprocal(
                rcol, sdcol), reads=[b_st], writes=[b_st])
            op("vector", lambda e, xnt=xnt, xct=xct, rcol=rcol: e.tensor_scalar(
                xnt[:, 0:1024], xct[:, 0:1024], rcol, None, ALU.mult),
                reads=[bxc, b_st], writes=[bxn])
            op("sync", lambda e, xnt=xnt, i=i: e.dma_start(
                out=xn_ag_in.ap()[128 * i:128 * (i + 1), :], in_=xnt[:, 0:1024]),
                reads=[bxn], writes=[d_xnagin], inc=16)

        bld.new_phase()
        bld.hard_sync = True
        arena_b.reset()
        arena_f.reset()
        xn_own = arena_b.alloc(NCH * 2048)
        KEEP_B = arena_b.mark
        comb_f = arena_f.alloc(16 * E * 4)
        KEEP_F = arena_f.mark
        wg_sb = arena_b.alloc(NCH * E)
        combt = arena_b.alloc(16 * 128)
        b2_sb = arena_b.alloc(1024)
        p3_x = [arena_f.alloc(1024) for _ in range(2)]
        bg_sb = arena_f.alloc(E)
        g_sc = arena_f.alloc(64)
        g_v1 = arena_f.alloc(E)
        g_v2 = arena_f.alloc(E)
        g_v3 = arena_f.alloc(E)

        b_xnown = bld.buf()
        b_wg, b_combt, b_b2 = bld.buf(), bld.buf(), bld.buf()
        b_bg = bld.buf()
        b_p3x = [bld.buf(), bld.buf()]
        b_comb = bld.buf()
        b_gsc, b_gv1, b_gv2, b_gv3 = bld.buf(), bld.buf(), bld.buf(), bld.buf()
        b_acc = Buf()

        op("gpsimd", lambda e: e.dma_start(out=wg_sb[:, 0:NCH * E], in_=wg_l),
           writes=[b_wg], inc=16)
        op("gpsimd", lambda e: e.dma_start(out=bg_sb[:, 0:E], in_=bg_bc_in),
           writes=[b_bg], inc=16)
        op("gpsimd", lambda e: e.dma_start(
            out=b2_sb.t[0:128, b2_sb.off:b2_sb.off + 1024], in_=b2_l),
           writes=[b_b2], inc=16)

        for c in range(NCH):
            for hh in range(2):
                op("sync", lambda e, c=c, hh=hh: e.dma_start_transpose(
                    xn_own[:, c * 2048 + hh * 1024:c * 2048 + (hh + 1) * 1024],
                    xn_ag_in.ap()[hh * 1024:(hh + 1) * 1024,
                                  128 * c:128 * (c + 1)]),
                   reads=[d_xnagin], writes=[b_xnown], inc=16)

        for i in range(16):
            gbank = 6
            fns = []
            for k in range(NCH):
                fns.append(lambda e, k=k, i=i: e.matmul(
                    ps[gbank][:, 0:E],
                    xn_own[:, k * 2048 + i * 128:k * 2048 + (i + 1) * 128],
                    wg_sb[:, k * E:(k + 1) * E],
                    start=(k == 0), stop=(k == NCH - 1)))
            group("tensor", fns, reads=[b_xnown, b_wg], writes=[psb[gbank]])
            lg = g_sc[:, 0:E]
            m1 = g_sc[:, 8:9]
            nm1 = g_sc[:, 12:13]
            m2 = g_sc[:, 16:17]
            dn = g_sc[:, 20:21]
            rd = g_sc[:, 24:25]
            op("vector", lambda e, lg=lg: e.tensor_tensor(
                lg, ps[6][:, 0:E], bg_sb[:, 0:E], ALU.add),
                reads=[psb[6], b_bg], writes=[b_gsc])
            op("vector", lambda e, m1=m1, lg=lg: e.reduce_max(m1, lg, axis=AX),
               reads=[b_gsc], writes=[b_gsc])
            op("vector", lambda e, nm1=nm1, m1=m1: e.tensor_scalar(
                nm1, m1, -1.0, None, ALU.mult), reads=[b_gsc], writes=[b_gsc])
            op("vector", lambda e, lg=lg, m1=m1: e.tensor_scalar(
                g_v1[:, 0:E], lg, m1, BIGNEG, ALU.is_equal, op1=ALU.mult),
                reads=[b_gsc], writes=[b_gv1])
            op("vector", lambda e, lg=lg: e.tensor_tensor(
                g_v2[:, 0:E], lg, g_v1[:, 0:E], ALU.add),
                reads=[b_gsc, b_gv1], writes=[b_gv2])
            op("vector", lambda e, m2=m2: e.reduce_max(
                m2, g_v2[:, 0:E], axis=AX), reads=[b_gv2], writes=[b_gsc])
            op("vector", lambda e, lg=lg, m2=m2: e.tensor_scalar(
                g_v1[:, 0:E], lg, m2, None, ALU.is_ge),
                reads=[b_gsc], writes=[b_gv1])
            op("scalar", lambda e, lg=lg, nm1=nm1: e.activation(
                out=g_v3[:, 0:E], in_=lg, func=AF.Exp, bias=nm1, scale=1.0),
                reads=[b_gsc], writes=[b_gv3])
            op("vector", lambda e: e.tensor_mul(
                g_v3[:, 0:E], g_v3[:, 0:E], g_v1[:, 0:E]),
                reads=[b_gv1, b_gv3], writes=[b_gv3])
            op("vector", lambda e, dn=dn: e.reduce_sum(
                dn, g_v3[:, 0:E], axis=AX), reads=[b_gv3], writes=[b_gsc])
            op("vector", lambda e, rd=rd, dn=dn: e.reciprocal(rd, dn),
               reads=[b_gsc], writes=[b_gsc])
            op("vector", lambda e, i=i, rd=rd: e.tensor_scalar(
                comb_f[:, i * E * 4:(i + 1) * E * 4:4], g_v3[:, 0:E], rd,
                None, ALU.mult),
                reads=[b_gv3, b_gsc], writes=[b_comb])
            op("tensor", lambda e, i=i: e.transpose(
                ps[7][0:E, 0:128],
                comb_f.t[:, comb_f.off + i * E * 4:
                         comb_f.off + (i + 1) * E * 4:4],
                ident[:, :]),
                reads=[b_comb, b_ident], writes=[psb[7]])
            op("vector", lambda e, i=i: e.tensor_copy(
                combt.t[0:E, combt.off + i * 128:combt.off + (i + 1) * 128],
                ps[7][0:E, 0:128]),
                reads=[psb[7]], writes=[b_combt])

        for i in range(16):
            sl = i % 2
            xt, bx = p3_x[sl], b_p3x[sl]
            op("sync", lambda e, i=i, xt=xt: e.dma_start(
                out=xt[:, 0:1024], in_=x_own[128 * i:128 * (i + 1), :]),
                writes=[bx], inc=16)
            for dh in range(2):
                bank = 2 + (i % 2)
                op("tensor", lambda e, i=i, dh=dh, bank=bank: e.matmul(
                    ps[bank][:, :],
                    combt.t[0:E, combt.off + i * 128:combt.off + (i + 1) * 128],
                    b2_sb.t[0:8, b2_sb.off + dh * 512:b2_sb.off + (dh + 1) * 512],
                    start=True, stop=True),
                    reads=[b_combt, b_b2], writes=[psb[bank]])
                op("vector", lambda e, i=i, dh=dh, bank=bank, xt=xt:
                   e.tensor_tensor(acc[:, i * 1024 + dh * 512:
                                       i * 1024 + (dh + 1) * 512],
                                   xt[:, dh * 512:(dh + 1) * 512],
                                   ps[bank][:, :], ALU.add),
                   reads=[bx, psb[bank]], writes=[b_acc])

        bld.hard_sync = False

        op("gpsimd", lambda e: e.collective_compute(
            "AllGather", ALU.bypass, replica_groups=[list(range(NCORES))],
            ins=[xn_ag_in.ap().opt()], outs=[xn_all.ap().opt()]),
            reads=[d_xnagin], writes=[d_xnall])

        bld.new_phase()
        arena_b.reset(to=KEEP_B)
        arena_f.reset(to=KEEP_F)
        xnq = [arena_b.alloc(NCH * 512) for _ in range(2)]
        k_rope = arena_b.alloc(4096)
        q_rope = arena_b.alloc(4096)
        v_tok = arena_b.alloc(32 * 130)
        w_q = arena_b.alloc(NCH * 128)
        w_qr = arena_b.alloc(NCH * 128)
        w_k = arena_b.alloc(NCH * 128)
        w_kr = arena_b.alloc(NCH * 128)
        w_v = arena_b.alloc(NCH * 128)
        probs = [arena_b.alloc(1024) for _ in range(4)]
        sqt_b = [arena_b.alloc(512) for _ in range(2)]
        ctx_stage = arena_b.alloc(512)
        cos_b = arena_b.alloc(S)
        sin_b = arena_b.alloc(S)
        mask_b = arena_b.alloc(4 * 1024)
        t1_f = arena_f.alloc(512)
        t2_f = arena_f.alloc(512)
        ctx_f = arena_f.alloc(512)
        rz_f = arena_f.alloc(32)

        b_xnq = [bld.buf(), bld.buf()]
        b_k = bld.buf()
        b_q = bld.buf()
        b_v = bld.buf()
        b_wq, b_wqr, b_wk, b_wkr, b_wv = (bld.buf(), bld.buf(), bld.buf(),
                                          bld.buf(), bld.buf())
        b_probs = [bld.buf() for _ in range(4)]
        b_sqt = [bld.buf(), bld.buf()]
        b_cstage = bld.buf()
        b_cos, b_sin, b_mask = bld.buf(), bld.buf(), bld.buf()
        b_t1, b_t2 = bld.buf(), bld.buf()
        b_ctxf = bld.buf()
        b_rz = bld.buf()
        b_negc = Buf()
        b_maxes = Buf()
        b_cstat = Buf()

        op("gpsimd", lambda e: e.dma_start(out=cos_b[:, 0:S], in_=cos_in),
           writes=[b_cos], inc=16)
        op("gpsimd", lambda e: e.dma_start(out=sin_b[:, 0:S], in_=sin_in),
           writes=[b_sin], inc=16)
        op("gpsimd", lambda e: e.dma_start(out=mask_b[:, 0:4096], in_=masks_in),
           writes=[b_mask], inc=16)
        op("gpsimd", lambda e: e.dma_start(out=w_q[:, 0:NCH * 128], in_=wq_p),
           writes=[b_wq], inc=16)
        op("gpsimd", lambda e: e.dma_start(out=w_k[:, 0:NCH * 128], in_=wk_p),
           writes=[b_wk], inc=16)
        op("gpsimd", lambda e: e.dma_start(out=w_v[:, 0:NCH * 128], in_=wv_p),
           writes=[b_wv], inc=16)
        for (wsrc, wdst, bs, bd) in ((w_q, w_qr, b_wq, b_wqr),
                                     (w_k, w_kr, b_wk, b_wkr)):
            n = NCH * 128
            op("vector", lambda e, wsrc=wsrc, wdst=wdst, n=n: e.tensor_scalar(
                wdst[:, 0:n - 1:2], wsrc[:, 1:n:2], -1.0, None, ALU.mult),
                reads=[bs], writes=[bd])
            op("vector", lambda e, wsrc=wsrc, wdst=wdst, n=n: e.tensor_copy(
                wdst[:, 1:n:2], wsrc[:, 0:n - 1:2]), reads=[bs], writes=[bd])

        op("vector", lambda e: e.memset(ctx_f.t[64:128,
                                                ctx_f.off:ctx_f.off + 512], 0.0),
           writes=[b_ctxf])
        op("vector", lambda e: e.memset(
            v_tok.t[:, v_tok.off + 64:v_tok.off + 32 * 130:130], 1.0),
           writes=[b_v])
        op("vector", lambda e: e.memset(
            v_tok.t[:, v_tok.off + 129:v_tok.off + 32 * 130:130], 1.0),
           writes=[b_v])

        ag1_tok = d_xnall.w

        for s in range(SEQS):
            for ttg in range(8):
                sl = ttg % 2
                xq, bq = xnq[sl], b_xnq[sl]
                t0 = s * 4096 + ttg * 512
                for c in range(NCH):
                    op("sync", lambda e, c=c, t0=t0, xq=xq:
                       e.dma_start_transpose(
                           xq[:, c * 512:(c + 1) * 512],
                           xn_all.ap()[t0:t0 + 512, 128 * c:128 * (c + 1)]),
                       writes=[bq], extra=[ag1_tok], inc=16)
                if True:
                    pos = ttg * 512
                    xsl = (lambda k, a, b2, xq=xq:
                           xq[:, k * 512 + a:k * 512 + b2])
                    for (wt, bw, bank) in ((w_k, b_wk, 0), (w_kr, b_wkr, 1),
                                           (w_q, b_wq, 2), (w_qr, b_wqr, 3)):
                        fns = []
                        for k in range(NCH):
                            fns.append(lambda e, wt=wt, k=k, bank=bank, xsl=xsl:
                                       e.matmul(ps[bank][:, :],
                                                wt[:, k * 128:(k + 1) * 128],
                                                xsl(k, 0, 512),
                                                start=(k == 0), stop=(k == NCH - 1)))
                        group("tensor", fns, reads=[bw, bq], writes=[psb[bank]])
                    for (bank_a, bank_r, dst, bd) in ((0, 1, k_rope, b_k),
                                                      (2, 3, q_rope, b_q)):
                        op("vector", lambda e, bank_a=bank_a, pos=pos: e.tensor_mul(
                            t1_f[:, 0:512], ps[bank_a][:, :],
                            cos_b[:, pos:pos + 512]),
                            reads=[psb[bank_a], b_cos], writes=[b_t1])
                        op("vector", lambda e, bank_r=bank_r, pos=pos: e.tensor_mul(
                            t2_f[:, 0:512], ps[bank_r][:, :],
                            sin_b[:, pos:pos + 512]),
                            reads=[psb[bank_r], b_sin], writes=[b_t2])
                        op("vector", lambda e, dst=dst, pos=pos: e.tensor_add(
                            dst[:, pos:pos + 512], t1_f[:, 0:512], t2_f[:, 0:512]),
                            reads=[b_t1, b_t2], writes=[bd])
                    for sub in range(4):
                        bank = 4 + (sub % 2)
                        fns = []
                        for k in range(NCH):
                            fns.append(lambda e, k=k, sub=sub, bank=bank, xsl=xsl:
                                       e.matmul(ps[bank][:, 0:128],
                                                xsl(k, sub * 128, (sub + 1) * 128),
                                                w_v[:, k * 128:(k + 1) * 128],
                                                start=(k == 0), stop=(k == NCH - 1)))
                        group("tensor", fns, reads=[b_wv, bq], writes=[psb[bank]])
                        kt = (pos + sub * 128) // 128
                        op("vector", lambda e, kt=kt, bank=bank: e.tensor_copy(
                            v_tok[:, kt * 130:kt * 130 + 64], ps[bank][:, 0:64]),
                            reads=[psb[bank]], writes=[b_v])
                        op("vector", lambda e, kt=kt, bank=bank: e.tensor_copy(
                            v_tok[:, kt * 130 + 65:kt * 130 + 129],
                            ps[bank][:, 64:128]),
                            reads=[psb[bank]], writes=[b_v])

            op("vector", lambda e: e.memset(maxes[:, :], 0.0), writes=[b_maxes])
            for (src, bs, col) in ((q_rope, b_q, 0), (k_rope, b_k, 2)):
                for i in range(8):
                    sl = i % 2
                    sq, bsq = sqt_b[sl], b_sqt[sl]
                    op("scalar", lambda e, src=src, i=i, sq=sq: e.activation(
                        out=sq[:, 0:512], in_=src[:, 512 * i:512 * (i + 1)],
                        func=AF.Square, bias=0.0, scale=1.0),
                        reads=[bs], writes=[bsq])
                    for hh in range(2):
                        bank = 6
                        op("tensor", lambda e, hh=hh, sq=sq: e.matmul(
                            ps[6][0:1, 0:512],
                            oneshd[:, hh:hh + 1], sq[:, 0:512],
                            start=True, stop=True),
                            reads=[b_oneshd, bsq], writes=[psb[6]])
                        op("vector", lambda e: e.reduce_max(
                            cstat[:, 0:1], ps[6][0:1, 0:512], axis=AX),
                            reads=[psb[6]], writes=[b_cstat])
                        cc = col + hh
                        op("vector", lambda e, cc=cc: e.tensor_max(
                            maxes[:, cc:cc + 1], maxes[:, cc:cc + 1],
                            cstat[:, 0:1]),
                            reads=[b_cstat, b_maxes], writes=[b_maxes])
            op("vector", lambda e: e.tensor_max(
                cstat[:, 1:2], maxes[:, 0:1], maxes[:, 1:2]),
                reads=[b_maxes], writes=[b_cstat])
            op("vector", lambda e: e.tensor_max(
                cstat[:, 2:3], maxes[:, 2:3], maxes[:, 3:4]),
                reads=[b_maxes, b_cstat], writes=[b_cstat])
            op("vector", lambda e: e.tensor_add(
                cstat[:, 3:4], cstat[:, 1:2], cstat[:, 2:3]),
                reads=[b_cstat], writes=[b_cstat])
            op("vector", lambda e: e.tensor_scalar(
                cstat[:, 4:5], cstat[:, 3:4], -0.0635, -1.0, ALU.mult,
                op1=ALU.add), reads=[b_cstat], writes=[b_cstat])
            op("tensor", lambda e: e.matmul(
                ps[6][0:128, 0:1], onesrow[0:1, 0:128], cstat[0:1, 4:5],
                start=True, stop=True),
                reads=[b_onesrow, b_cstat], writes=[psb[6]])
            op("vector", lambda e: e.tensor_copy(negc_bc[:, 0:1], ps[6][:, 0:1]),
               reads=[psb[6]], writes=[b_negc])

            for qp in range(4):
                qts = [2 * qp, 2 * qp + 1]
                nkts = [4 * (qts[0] + 1), 4 * (qts[1] + 1)]

                def c_scores(ci, kt):
                    qt, nkt = qts[ci], nkts[ci]
                    sbank = [2 * ci, 2 * ci + 1]
                    for h in range(2):
                        op("tensor", lambda e, h=h, kt=kt, qt=qt, sbank=sbank:
                           e.matmul(ps[sbank[h]][:, :],
                                    k_rope[64 * h:64 * (h + 1),
                                           kt * 128:(kt + 1) * 128],
                                    q_rope[64 * h:64 * (h + 1),
                                           qt * 512:(qt + 1) * 512],
                                    start=True, stop=True,
                                    tile_position=(64 * h, 0)),
                           reads=[b_k, b_q], writes=[psb[sbank[h]]])
                    if kt >= nkt - 4:
                        d_idx = kt - (nkt - 4)
                        op("vector", lambda e, d_idx=d_idx, ci=ci:
                           e.tensor_tensor(pp[2 * ci][:, :],
                                           pp[2 * ci][:, :],
                                           mask_b[:, d_idx * 1024:
                                                  (d_idx + 1) * 1024],
                                           ALU.add),
                           reads=[b_mask, psb[sbank[0]], psb[sbank[1]]],
                           writes=[psb[sbank[0]], psb[sbank[1]]])
                    pslot = 2 * ci + (kt % 2)
                    op("scalar", lambda e, ci=ci, pslot=pslot:
                       e.activation(out=probs[pslot][:, 0:1024],
                                    in_=pp[2 * ci][:, :], func=AF.Exp,
                                    bias=negc_bc[:, 0:1], scale=0.125),
                       reads=[psb[sbank[0]], psb[sbank[1]], b_negc],
                       writes=[b_probs[pslot]])

                def c_av(ci, kt):
                    nkt = nkts[ci]
                    cbank = [4 + 2 * ci, 5 + 2 * ci]
                    pslot = 2 * ci + (kt % 2)
                    for h in range(2):
                        op("tensor", lambda e, h=h, kt=kt, pslot=pslot,
                           cbank=cbank, nkt=nkt:
                           e.matmul(ps[cbank[h]][0:65, :],
                                    v_tok[:, kt * 130 + 65 * h:
                                          kt * 130 + 65 * h + 65],
                                    probs[pslot][:, h * 512:(h + 1) * 512],
                                    start=(kt == 0), stop=(kt == nkt - 1)),
                           reads=[b_v, b_probs[pslot]],
                           writes=[psb[cbank[h]]], inc=1)

                for step in range(max(nkts) + 1):
                    for ci in range(2):
                        if step < nkts[ci]:
                            c_scores(ci, step)
                    for ci in range(2):
                        if 1 <= step <= nkts[ci]:
                            c_av(ci, step - 1)

                for ci in range(2):
                    qt = qts[ci]
                    cbank = [4 + 2 * ci, 5 + 2 * ci]
                    cst, bcst = ctx_stage, b_cstage
                    for h in range(2):
                        op("vector", lambda e, h=h, cbank=cbank: e.tensor_copy(
                            ctx_f.t[0:65, ctx_f.off:ctx_f.off + 512],
                            ps[cbank[h]][0:65, :]),
                            reads=[psb[cbank[h]]], writes=[b_ctxf])
                        tfns = []
                        for sub in range(4):
                            tfns.append(lambda e, sub=sub, h=h, cbank=cbank:
                                        e.transpose(
                                            ps[cbank[h]][:, sub * 128:
                                                         (sub + 1) * 128],
                                            ctx_f.t[:, ctx_f.off + sub * 128:
                                                    ctx_f.off + (sub + 1) * 128],
                                            ident[:, :]))
                        group("tensor", tfns, reads=[b_ctxf, b_ident],
                              writes=[psb[cbank[h]]])
                        for sub in range(4):
                            op("vector", lambda e, sub=sub, h=h, cbank=cbank:
                               e.reciprocal(rz_f[:, sub * 4:sub * 4 + 1],
                                            ps[cbank[h]][:, sub * 128 + 64:
                                                         sub * 128 + 65]),
                               reads=[psb[cbank[h]]], writes=[b_rz])
                            op("vector", lambda e, sub=sub, h=h, cbank=cbank,
                               cst=cst:
                               e.tensor_scalar(
                                   cst[:, sub * 128 + 64 * h:
                                       sub * 128 + 64 * h + 64],
                                   ps[cbank[h]][:, sub * 128:sub * 128 + 64],
                                   rz_f[:, sub * 4:sub * 4 + 1], None, ALU.mult),
                               reads=[psb[cbank[h]], b_rz], writes=[bcst])
                    t0 = s * 4096 + qt * 512
                    op("sync", lambda e, t0=t0, cst=cst: e.dma_start(
                        out=ctx_ag_in.ap()[t0:t0 + 512, :].rearrange(
                            "(sub p) c -> p sub c", p=128),
                        in_=cst[:, 0:512].rearrange("p (sub c) -> p sub c",
                                                    c=128)),
                        reads=[bcst], writes=[d_ctxagin], inc=16)

        bld.new_phase()
        bld.hard_sync = True
        bld.window = 8
        bld.hist = []
        arena_b.reset(to=KEEP_B)
        arena_f.reset(to=KEEP_F)
        w1_sb = arena_b.alloc(16 * 8 * 128)
        w2_sb = arena_b.alloc(16 * 1024)
        h_sb = arena_b.alloc(16 * 512)
        ysc = [arena_f.alloc(1024) for _ in range(2)]
        b1_sb = arena_f.alloc(E * 16 * 4)
        b_w1h = [bld.buf(), bld.buf()]
        b_w2 = bld.buf()
        b_h = bld.buf()
        b_b1 = bld.buf()
        b_ysc = [bld.buf(), bld.buf()]

        op("gpsimd", lambda e: e.dma_start(out=b1_sb[:, 0:E * 16 * 4], in_=b1_l),
           writes=[b_b1], inc=16)
        for hf in range(2):
            op("gpsimd", lambda e, hf=hf: e.dma_start(
                out=w1_sb[:, hf * 8192:(hf + 1) * 8192],
                in_=w1_l[0, :, hf * 8192:(hf + 1) * 8192]),
                writes=[b_w1h[hf]], inc=16)
        op("gpsimd", lambda e: e.dma_start(
            out=w2_sb[:, 0:16384], in_=w2_l[0, :, :]), writes=[b_w2], inc=16)

        for e_i in range(E):
            for tt in range(4):
                tt_fence = [(en, bld.S.count[en]) for en in ENGINES
                            if en != "gpsimd" and bld.S.count[en] > 0]
                for j in range(16):
                    bank = j % 2
                    fns = []
                    for k in range(NCH):
                        fns.append(lambda e, j=j, k=k, tt=tt, bank=bank:
                                   e.matmul(ps[bank][:, :],
                                            w1_sb[:, (j * 8 + k) * 128:
                                                  (j * 8 + k + 1) * 128],
                                            xn_own[:, k * 2048 + tt * 512:
                                                   k * 2048 + (tt + 1) * 512],
                                            start=(k == 0), stop=(k == NCH - 1)))
                    group("tensor", fns, reads=[b_w1h[j // 8], b_xnown],
                          writes=[psb[bank]],
                          extra=(tt_fence if j == 0 else ()))
                    op("scalar", lambda e, j=j, e_i=e_i, bank=bank: e.activation(
                        out=h_sb[:, j * 512:(j + 1) * 512], in_=ps[bank][:, :],
                        func=AF.Relu,
                        bias=b1_sb[:, (e_i * 16 + j) * 4:(e_i * 16 + j) * 4 + 1],
                        scale=1.0),
                        reads=[psb[bank], b_b1], writes=[b_h])
                for u in range(4):
                    i = tt * 4 + u
                    for dh in range(2):
                        bank = 2 + dh
                        fns = []
                        for j in range(16):
                            fns.append(lambda e, j=j, u=u, dh=dh, bank=bank:
                                       e.matmul(ps[bank][:, :],
                                                h_sb[:, j * 512 + u * 128:
                                                     j * 512 + (u + 1) * 128],
                                                w2_sb[:, j * 1024 + dh * 512:
                                                      j * 1024 + (dh + 1) * 512],
                                                start=(j == 0), stop=(j == 15)))
                        group("tensor", fns, reads=[b_h, b_w2],
                              writes=[psb[bank]])
                    ysl = u % 2
                    op("scalar", lambda e, i=i, e_i=e_i, ysl=ysl:
                       e.activation(out=ysc[ysl][:, 0:1024],
                                    in_=pp23[:, :], func=AF.Copy,
                                    bias=0.0,
                                    scale=comb_f[:, (i * E + e_i) * 4:
                                                 (i * E + e_i) * 4 + 1]),
                       reads=[psb[2], psb[3], b_comb], writes=[b_ysc[ysl]])
                    op("vector", lambda e, i=i, ysl=ysl:
                       e.tensor_add(
                           acc[:, i * 1024:(i + 1) * 1024],
                           acc[:, i * 1024:(i + 1) * 1024],
                           ysc[ysl][:, 0:1024]),
                       reads=[b_ysc[ysl], b_acc], writes=[b_acc])
            if e_i + 1 < E:
                bld.hard_sync = False
                for hf in range(2):
                    op("gpsimd", lambda e, e_i=e_i, hf=hf: e.dma_start(
                        out=w1_sb[:, hf * 8192:(hf + 1) * 8192],
                        in_=w1_l[e_i + 1, :, hf * 8192:(hf + 1) * 8192]),
                        writes=[b_w1h[hf]], inc=16)
                op("gpsimd", lambda e, e_i=e_i: e.dma_start(
                    out=w2_sb[:, 0:16384], in_=w2_l[e_i + 1, :, :]),
                    writes=[b_w2], inc=16)
                bld.hard_sync = True

        bld.hard_sync = False
        bld.window = 0
        op("gpsimd", lambda e: e.collective_compute(
            "AllGather", ALU.bypass, replica_groups=[list(range(NCORES))],
            ins=[ctx_ag_in.ap().opt()], outs=[ctx_all.ap().opt()]),
            reads=[d_ctxagin], writes=[d_ctxall])

        bld.new_phase()
        arena_b.reset()
        arena_f.reset()
        ctx_fm = arena_b.alloc(NCORES * 2048)
        wo_sb = arena_b.alloc(NCORES * 1024)
        p4_out = [arena_f.alloc(512) for _ in range(2)]
        b_ctxfm = bld.buf()
        b_wo = bld.buf()
        b_out = [bld.buf(), bld.buf()]

        op("gpsimd", lambda e: e.dma_start(out=wo_sb[:, 0:NCORES * 1024],
                                           in_=wo_l), writes=[b_wo], inc=16)

        def gather_fn(e):
            with e.register("goff") as goff:
                pid = e.partition_id()
                e.reg_mov(goff, pid)
                e.reg_mul(goff, goff, OWN * 128)
                src = bass.AP(ctx_all, goff,
                              [[NTOK * 128, NCORES], [128, OWN], [1, 128]])
                dst = bass.AP(ctx_mine, 0,
                              [[OWN * 128, NCORES], [128, OWN], [1, 128]])
                return e.dma_start(out=dst, in_=src)
        op("gpsimd", gather_fn, reads=[d_ctxall], writes=[d_ctxmine], inc=16)

        for r in range(NCORES):
            op("sync", lambda e, r=r: e.dma_start_transpose(
                ctx_fm[:, r * 2048:(r + 1) * 2048],
                ctx_mine.ap()[r * OWN:(r + 1) * OWN, :]),
                reads=[d_ctxmine], writes=[b_ctxfm], inc=16)

        for i in range(16):
            for dh in range(2):
                bank = (i * 2 + dh) % 4
                fns = []
                for r in range(NCORES):
                    fns.append(lambda e, r=r, i=i, dh=dh, bank=bank:
                               e.matmul(ps[bank][:, :],
                                        ctx_fm[:, r * 2048 + i * 128:
                                               r * 2048 + (i + 1) * 128],
                                        wo_sb[:, r * 1024 + dh * 512:
                                              r * 1024 + (dh + 1) * 512],
                                        start=(r == 0), stop=(r == NCORES - 1)))
                group("tensor", fns, reads=[b_ctxfm, b_wo], writes=[psb[bank]])
                sl = (i * 2 + dh) % 2
                ot, bo = p4_out[sl], b_out[sl]
                op("vector", lambda e, i=i, dh=dh, bank=bank, ot=ot:
                   e.tensor_add(ot[:, 0:512], ps[bank][:, :],
                                acc[:, i * 1024 + dh * 512:
                                    i * 1024 + (dh + 1) * 512]),
                   reads=[psb[bank], b_acc], writes=[bo])
                op("sync", lambda e, i=i, dh=dh, ot=ot: e.dma_start(
                    out=y_own[i * 128:(i + 1) * 128, dh * 512:(dh + 1) * 512],
                    in_=ot[:, 0:512]),
                    reads=[bo], writes=[d_yout], inc=16)

        block = ctx.enter_context(nc.Block())
        final_sync = bld.S.count["sync"]

        def make_body(eng_name):
            def body(eng):
                last_wait = {e: 0 for e in ENGINES}
                for wd, fn, inc in bld.S.events[eng_name]:
                    for we, wv in wd.items():
                        if wv > last_wait[we]:
                            eng.wait_ge(sem[we], wv)
                            last_wait[we] = wv
                    inst = fn(eng)
                    if inc and inst is not None:
                        inst.then_inc(sem[eng_name], inc)
                if eng_name == "sync":
                    eng.wait_ge(sem["sync"], final_sync)
            return body

        block.tensor(make_body("tensor"))
        block.vector(make_body("vector"))
        block.scalar(make_body("scalar"))
        block.gpsimd(make_body("gpsimd"))
        block.sync(make_body("sync"))

    return nc


def _host_inputs(L0_attn_x, Wq, Wk, Wv, Wo, Wg, bg, W1, b1, W2, b2):
    f32 = np.float32
    x = np.ascontiguousarray(np.asarray(L0_attn_x, dtype=f32)).reshape(NTOK, D)

    def pair_slice(W, c):
        sl = np.asarray(W[:, 128 * c:128 * (c + 1)], dtype=f32)
        return np.ascontiguousarray(sl.reshape(NCH, 128, 128).transpose(1, 0, 2)
                                    .reshape(128, NCH * 128))

    Wo = np.asarray(Wo, dtype=f32)
    wo_l = np.ascontiguousarray(Wo.reshape(NCORES, 128, 1024)
                                .transpose(1, 0, 2).reshape(128, NCORES * 1024))
    Wg = np.asarray(Wg, dtype=f32)
    wg_l = np.ascontiguousarray(Wg.reshape(NCH, 128, E).transpose(1, 0, 2)
                                .reshape(128, NCH * E))
    bg_bc = np.ascontiguousarray(np.broadcast_to(np.asarray(bg, dtype=f32),
                                                 (128, E)))
    W1 = np.asarray(W1, dtype=f32)
    w1_l = np.ascontiguousarray(W1.reshape(E, NCH, 128, 16, 128)
                                .transpose(0, 2, 3, 1, 4)
                                .reshape(E, 128, 16 * NCH * 128))
    b1 = np.asarray(b1, dtype=f32)
    _b1c = b1.reshape(E, 16, 128).transpose(2, 0, 1).reshape(128, E * 16)
    b1_l = np.zeros((128, E * 16 * 4), dtype=f32)
    b1_l[:, 0::4] = _b1c
    b1_l = np.ascontiguousarray(b1_l)
    W2 = np.asarray(W2, dtype=f32)
    w2_l = np.ascontiguousarray(W2.reshape(E, 16, 128, 1024)
                                .transpose(0, 2, 1, 3).reshape(E, 128, 16 * 1024))
    _b2 = np.asarray(b2, dtype=f32)
    b2_l = np.zeros((128, D), dtype=f32)
    b2_l[0:E] = _b2
    b2_l = np.ascontiguousarray(b2_l)

    pos = np.arange(S, dtype=np.float64)
    d_idx = np.arange(128) % 64
    inv_freq = 1.0 / (10000.0 ** ((2 * (d_idx // 2)).astype(np.float64) / DK))
    theta = inv_freq[:, None] * pos[None, :]
    cos_t = np.cos(theta).astype(f32)
    sin_t = np.sin(theta).astype(f32)

    masks = np.zeros((128, 4, 2, 512), dtype=f32)
    ii = np.arange(128)[:, None]
    qq = np.arange(512)[None, :]
    for di, dd in enumerate((0, 128, 256, 384)):
        m = np.where(dd + ii > qq, BIGNEG, 0.0)
        masks[:, di, 0, :] = m
        masks[:, di, 1, :] = m
    masks = np.ascontiguousarray(masks.reshape(128, 4 * 1024))

    ident = np.eye(128, dtype=f32)
    oneshd = np.zeros((128, 2), dtype=f32)
    oneshd[0:64, 0] = 1.0
    oneshd[64:128, 1] = 1.0
    onesrow = np.ones((128, 128), dtype=f32)

    Wq = np.asarray(Wq, dtype=f32)
    Wk = np.asarray(Wk, dtype=f32)
    Wv = np.asarray(Wv, dtype=f32)

    in_maps = []
    for c in range(NCORES):
        in_maps.append({
            "x_own": np.ascontiguousarray(x[OWN * c:OWN * (c + 1)]),
            "wq_p": pair_slice(Wq, c),
            "wk_p": pair_slice(Wk, c),
            "wv_p": pair_slice(Wv, c),
            "wo_l": wo_l,
            "wg_l": wg_l,
            "bg_bc": bg_bc,
            "w1_l": w1_l,
            "b1_l": b1_l,
            "w2_l": w2_l,
            "b2_l": b2_l,
            "cos_t": cos_t,
            "sin_t": sin_t,
            "masks": masks,
            "ident": ident,
            "oneshd": oneshd,
            "onesrow": onesrow,
        })
    return in_maps


_NC_CACHE = None


def kernel(L0_attn_x, L0_attn_Wq, L0_attn_Wk, L0_attn_Wv, L0_attn_Wo,
           L0_ffn_Wg, L0_ffn_bg, L0_ffn_W1, L0_ffn_b1, L0_ffn_W2, L0_ffn_b2):
    global _NC_CACHE
    in_maps = _host_inputs(L0_attn_x, L0_attn_Wq, L0_attn_Wk, L0_attn_Wv,
                           L0_attn_Wo, L0_ffn_Wg, L0_ffn_bg, L0_ffn_W1,
                           L0_ffn_b1, L0_ffn_W2, L0_ffn_b2)
    if _NC_CACHE is None:
        _NC_CACHE = build_program()
    res = run_bass_kernel_spmd(_NC_CACHE, in_maps, core_ids=list(range(NCORES)))
    out = np.concatenate([res.results[c]["y_own"] for c in range(NCORES)],
                         axis=0)
    return out.reshape(B, S, D).astype(np.float32)


# version 13 (final)
# speedup vs baseline: 1.0147x; 1.0147x over previous
import numpy as np
import ml_dtypes
from contextlib import ExitStack

import concourse.bass as bass
import concourse.mybir as mybir
from concourse.bass_utils import run_bass_kernel_spmd

F32 = mybir.dt.float32
BF16 = mybir.dt.bfloat16
AF = mybir.ActivationFunctionType
ALU = mybir.AluOpType
AX = mybir.AxisListType.X

NCORES = 8
B, S, D = 4, 4096, 1024
NTOK = B * S
OWN = NTOK // NCORES
H, DK = 16, 64
E, DH = 8, 2048
NCH = D // 128
SEQS = B

BIGNEG = -1.0e9


ENGINES = ("tensor", "vector", "scalar", "gpsimd", "sync")


class Sched:
    def __init__(self):
        self.events = {e: [] for e in ENGINES}
        self.count = {e: 0 for e in ENGINES}

    def emit(self, eng, fn, waits=(), inc=0):
        wd = {}
        for t in waits:
            if t is None:
                continue
            we, wv = t
            if wv > wd.get(we, 0):
                wd[we] = wv
        if inc:
            self.count[eng] += inc
        tok = (eng, self.count[eng]) if inc else None
        self.events[eng].append((wd, fn, inc))
        return tok


class Buf:

    __slots__ = ("w", "rs")

    def __init__(self):
        self.w = None
        self.rs = []

    def rdeps(self):
        return [self.w] if self.w else []

    def wdeps(self):
        return ([self.w] if self.w else []) + self.rs


class Builder:
    def __init__(self, nc):
        self.nc = nc
        self.S = Sched()
        self.fence = []
        self.window = 0
        self.hist = []

    def new_phase(self):
        self.fence = [(e, self.S.count[e]) for e in ENGINES
                      if self.S.count[e] > 0]

    def buf(self):
        b = Buf()
        b.rs = list(self.fence)
        return b

    def _hs_fence(self):
        snap = [(e, self.S.count[e]) for e in ENGINES if self.S.count[e] > 0]
        if not self.window:
            return snap
        self.hist.append(snap)
        idx = len(self.hist) - 1 - self.window
        return self.hist[idx] if idx >= 0 else []

    def op(self, eng, fn, reads=(), writes=(), extra=(), inc=1):
        waits = []
        if getattr(self, "hard_sync", False):
            waits += self._hs_fence()
        for b in reads:
            waits += b.rdeps()
        for b in writes:
            waits += b.wdeps()
        waits += [t for t in extra if t is not None]
        tok = self.S.emit(eng, fn, waits, inc)
        for b in reads:
            b.rs.append(tok)
        for b in writes:
            b.w = tok
            b.rs = []
        return tok

    def group(self, eng, fns, reads=(), writes=(), extra=(), inc=1):
        waits = []
        if getattr(self, "hard_sync", False):
            waits += self._hs_fence()
        for b in reads:
            waits += b.rdeps()
        for b in writes:
            waits += b.wdeps()
        waits += [t for t in extra if t is not None]
        tok = None
        for i, fn in enumerate(fns):
            last = i == len(fns) - 1
            tok2 = self.S.emit(eng, fn, waits if i == 0 else (), inc if last else 0)
            if last:
                tok = tok2
        for b in reads:
            b.rs.append(tok)
        for b in writes:
            b.w = tok
            b.rs = []
        return tok


class Tile:

    def __init__(self, t, off, n):
        self.t = t
        self.off = off
        self.n = n

    def __getitem__(self, idx):
        p, f = idx
        start = self.off + (f.start or 0)
        stop = self.off + (f.stop if f.stop is not None else self.n)
        if f.step in (None, 1):
            return self.t[p, start:stop]
        return self.t[p, start:stop:f.step]


class PTile:

    def __init__(self, t, off, n):
        self.t = t
        self.off = off
        self.n = n

    def __getitem__(self, idx):
        p, f = idx
        start = self.off + (f.start or 0)
        stop = self.off + (f.stop if f.stop is not None else self.n)
        return self.t[p, start:stop]


class Arena:
    def __init__(self, nc, name, n_elems, dtype):
        self._guard = nc.sbuf_tensor(name, [128, n_elems], dtype)
        self.t = None
        self.n = n_elems
        self.mark = 0

    def enter(self, ctx):
        self.t = ctx.enter_context(self._guard)

    def reset(self, to=0):
        self.mark = to

    def alloc(self, n):
        off = (self.mark + 15) // 16 * 16
        self.mark = off + n
        assert self.mark <= self.n, f"arena overflow: {self.mark} > {self.n}"
        return Tile(self.t, off, n)


def build_program():
    nc = bass.Bass("TRN2", target_bir_lowering=False)

    x_own = nc.dram_tensor("x_own", [OWN, D], F32, kind="ExternalInput").ap()
    wq_p = nc.dram_tensor("wq_p", [128, NCH * 128], F32, kind="ExternalInput").ap()
    wk_p = nc.dram_tensor("wk_p", [128, NCH * 128], F32, kind="ExternalInput").ap()
    wv_p = nc.dram_tensor("wv_p", [128, NCH * 128], F32, kind="ExternalInput").ap()
    wo_l = nc.dram_tensor("wo_l", [128, NCORES * 1024], F32, kind="ExternalInput").ap()
    wg_l = nc.dram_tensor("wg_l", [128, NCH * E], F32, kind="ExternalInput").ap()
    bg_bc_in = nc.dram_tensor("bg_bc", [128, E], F32, kind="ExternalInput").ap()
    w1_l = nc.dram_tensor("w1_l", [E, 128, 16 * 8 * 128], F32, kind="ExternalInput").ap()
    b1_l = nc.dram_tensor("b1_l", [128, E * 16 * 4], F32, kind="ExternalInput").ap()
    w2_l = nc.dram_tensor("w2_l", [E, 128, 16 * 1024], F32, kind="ExternalInput").ap()
    b2_l = nc.dram_tensor("b2_l", [128, D], F32, kind="ExternalInput").ap()
    cos_in = nc.dram_tensor("cos_t", [128, S], F32, kind="ExternalInput").ap()
    sin_in = nc.dram_tensor("sin_t", [128, S], F32, kind="ExternalInput").ap()
    masks_in = nc.dram_tensor("masks", [128, 4 * 1024], F32, kind="ExternalInput").ap()
    ident_in = nc.dram_tensor("ident", [128, 128], F32, kind="ExternalInput").ap()
    oneshd_in = nc.dram_tensor("oneshd", [128, 2], F32, kind="ExternalInput").ap()
    onesrow_in = nc.dram_tensor("onesrow", [128, 128], F32, kind="ExternalInput").ap()

    y_own = nc.dram_tensor("y_own", [OWN, D], F32, kind="ExternalOutput").ap()

    xn_ag_in = nc.dram_tensor("xn_ag_in", [OWN, D], BF16, kind="Internal")
    xn_all_h = [nc.dram_tensor(f"xn_all{h}", [NTOK // 2, D], BF16,
                               kind="Internal") for h in range(2)]
    ctx_ag_in = nc.dram_tensor("ctx_ag_in", [NTOK, 128], BF16, kind="Internal")
    ctx_all = nc.dram_tensor("ctx_all", [NCORES * NTOK, 128], BF16, kind="Internal")
    ctx_mine = nc.dram_tensor("ctx_mine", [NCORES * OWN, 128], BF16, kind="Internal")

    with ExitStack() as ctx:
        bld = Builder(nc)
        op, group = bld.op, bld.group

        def sbuf(name, shape, dtype):
            return ctx.enter_context(nc.sbuf_tensor(name, shape, dtype))

        acc = sbuf("acc", [128, 16 * 1024], F32)
        ident = sbuf("ident_s", [128, 128], F32)
        oneshd = sbuf("oneshd_s", [128, 2], BF16)
        onesrow = sbuf("onesrow_s", [128, 128], F32)
        negc_bc = sbuf("negc_bc", [128, 1], F32)
        cstat = sbuf("cstat", [1, 16], F32)
        maxes = sbuf("maxes", [1, 4], F32)

        arena_b = Arena(nc, "arena_b", 60544, BF16)
        arena_f = Arena(nc, "arena_f", 6144, F32)
        arena_b.enter(ctx)
        arena_f.enter(ctx)

        pp01 = ctx.enter_context(nc.psum_tensor("pp01", [128, 1024], F32))
        pp23 = ctx.enter_context(nc.psum_tensor("pp23", [128, 1024], F32))
        ps_s = [ctx.enter_context(nc.psum_tensor(f"ps{i}", [128, 512], F32))
                for i in range(4, 8)]
        ps = [PTile(pp01, 0, 512), PTile(pp01, 512, 512),
              PTile(pp23, 0, 512), PTile(pp23, 512, 512),
              PTile(ps_s[0], 0, 512), PTile(ps_s[1], 0, 512),
              PTile(ps_s[2], 0, 512), PTile(ps_s[3], 0, 512)]
        psb = [Buf() for _ in range(8)]
        pp = [pp01, pp01, pp23, pp23]

        sem = {e: ctx.enter_context(nc.semaphore("sem_" + e)) for e in ENGINES}

        d_xnagin_h = [Buf(), Buf()]
        d_xnall_h = [Buf(), Buf()]
        d_ctxagin = Buf()
        d_ctxall = Buf()
        d_ctxmine = Buf()
        d_yout = Buf()

        b_ident = Buf()
        b_oneshd = Buf()
        b_onesrow = Buf()
        op("sync", lambda e: e.dma_start(out=ident[:, :], in_=ident_in),
           writes=[b_ident], inc=16)
        op("gpsimd", lambda e: e.dma_start(out=oneshd[:, :], in_=oneshd_in),
           writes=[b_oneshd], inc=16)
        op("sync", lambda e: e.dma_start(out=onesrow[:, :], in_=onesrow_in),
           writes=[b_onesrow], inc=16)



        arena_b.reset()
        arena_f.reset()
        p1_xn = [arena_b.alloc(1024) for _ in range(2)]
        p1_x = [arena_f.alloc(1024) for _ in range(2)]
        p1_xc = [arena_f.alloc(1024) for _ in range(2)]
        _p1sq = arena_f.alloc(1024)
        p1_sq = [_p1sq, _p1sq]
        p1_st = arena_f.alloc(5 * 16 * 4)
        b_xn = [Buf(), Buf()]
        b_x = [Buf(), Buf()]
        b_xc = [Buf(), Buf()]
        _bsq = Buf()
        b_sq = [_bsq, _bsq]
        b_st = Buf()

        for i in range(16):
            sl = i % 2
            xt, xct, sqt, xnt = p1_x[sl], p1_xc[sl], p1_sq[sl], p1_xn[sl]
            bx, bxc, bsq, bxn = b_x[sl], b_xc[sl], b_sq[sl], b_xn[sl]
            mcol = p1_st[:, 4 * (5 * i + 0): 4 * (5 * i + 0) + 1]
            vcol = p1_st[:, 4 * (5 * i + 1): 4 * (5 * i + 1) + 1]
            sdcol = p1_st[:, 4 * (5 * i + 2): 4 * (5 * i + 2) + 1]
            rcol = p1_st[:, 4 * (5 * i + 3): 4 * (5 * i + 3) + 1]
            scol = p1_st[:, 4 * (5 * i + 4): 4 * (5 * i + 4) + 1]

            op("sync", lambda e, xt=xt, i=i: e.dma_start(
                out=xt[:, 0:1024], in_=x_own[128 * i:128 * (i + 1), :]),
                writes=[bx], inc=16)
            op("vector", lambda e, xt=xt, scol=scol: e.reduce_sum(
                scol, xt[:, 0:1024], axis=AX), reads=[bx], writes=[b_st])
            op("vector", lambda e, mcol=mcol, scol=scol: e.tensor_scalar(
                mcol, scol, 1.0 / D, None, ALU.mult), reads=[b_st], writes=[b_st])
            op("vector", lambda e, xct=xct, xt=xt, mcol=mcol: e.tensor_scalar(
                xct[:, 0:1024], xt[:, 0:1024], mcol, None, ALU.subtract),
                reads=[bx, b_st], writes=[bxc])
            op("vector", lambda e, sqt=sqt, xct=xct: e.tensor_mul(
                sqt[:, 0:1024], xct[:, 0:1024], xct[:, 0:1024]),
                reads=[bxc], writes=[bsq])
            op("vector", lambda e, vcol=vcol, sqt=sqt: e.reduce_sum(
                vcol, sqt[:, 0:1024], axis=AX), reads=[bsq], writes=[b_st])
            op("vector", lambda e, vcol=vcol: e.tensor_scalar(
                vcol, vcol, 1.0 / D, 1e-5, ALU.mult, op1=ALU.add),
                reads=[b_st], writes=[b_st])
            op("scalar", lambda e, sdcol=sdcol, vcol=vcol: e.activation(
                out=sdcol, in_=vcol, func=AF.Sqrt, bias=0.0, scale=1.0),
                reads=[b_st], writes=[b_st])
            op("vector", lambda e, rcol=rcol, sdcol=sdcol: e.reciprocal(
                rcol, sdcol), reads=[b_st], writes=[b_st])
            op("vector", lambda e, xnt=xnt, xct=xct, rcol=rcol: e.tensor_scalar(
                xnt[:, 0:1024], xct[:, 0:1024], rcol, None, ALU.mult),
                reads=[bxc, b_st], writes=[bxn])
            op("sync", lambda e, xnt=xnt, i=i: e.dma_start(
                out=xn_ag_in.ap()[128 * i:128 * (i + 1), :], in_=xnt[:, 0:1024]),
                reads=[bxn], writes=[d_xnagin_h[i // 8]], inc=16)

        bld.new_phase()
        arena_b.reset()
        arena_f.reset()
        xn_own = arena_b.alloc(NCH * 2048)
        KEEP_B = arena_b.mark
        comb_f = arena_f.alloc(16 * E * 4)
        KEEP_F = arena_f.mark

        xnq = [arena_b.alloc(NCH * 512) for _ in range(2)]
        k_rope = arena_b.alloc(4096)
        q_rope = arena_b.alloc(4096)
        v_tok = arena_b.alloc(32 * 130)
        w_q = arena_b.alloc(NCH * 128)
        w_qr = arena_b.alloc(NCH * 128)
        w_k = arena_b.alloc(NCH * 128)
        w_kr = arena_b.alloc(NCH * 128)
        w_v = arena_b.alloc(NCH * 128)
        probs = [arena_b.alloc(1024) for _ in range(4)]
        sqt_b = [arena_b.alloc(512) for _ in range(2)]
        ctx_stage = arena_b.alloc(512)
        cos_b = arena_b.alloc(S)
        sin_b = arena_b.alloc(S)
        mask_b = arena_b.alloc(4 * 1024)
        t1_f = arena_f.alloc(512)
        t2_f = arena_f.alloc(512)
        ctx_f = arena_f.alloc(512)
        rz_f = arena_f.alloc(32)

        b_wq, b_wqr, b_wk, b_wkr, b_wv = Buf(), Buf(), Buf(), Buf(), Buf()
        b_cos, b_sin, b_mask = Buf(), Buf(), Buf()
        op("gpsimd", lambda e: e.dma_start(out=cos_b[:, 0:S], in_=cos_in),
           writes=[b_cos], inc=16)
        op("gpsimd", lambda e: e.dma_start(out=sin_b[:, 0:S], in_=sin_in),
           writes=[b_sin], inc=16)
        op("gpsimd", lambda e: e.dma_start(out=mask_b[:, 0:4096], in_=masks_in),
           writes=[b_mask], inc=16)
        op("gpsimd", lambda e: e.dma_start(out=w_q[:, 0:NCH * 128], in_=wq_p),
           writes=[b_wq], inc=16)
        op("gpsimd", lambda e: e.dma_start(out=w_k[:, 0:NCH * 128], in_=wk_p),
           writes=[b_wk], inc=16)
        op("gpsimd", lambda e: e.dma_start(out=w_v[:, 0:NCH * 128], in_=wv_p),
           writes=[b_wv], inc=16)
        for (wsrc, wdst, bs, bd) in ((w_q, w_qr, b_wq, b_wqr),
                                     (w_k, w_kr, b_wk, b_wkr)):
            n = NCH * 128
            op("vector", lambda e, wsrc=wsrc, wdst=wdst, n=n: e.tensor_scalar(
                wdst[:, 0:n - 1:2], wsrc[:, 1:n:2], -1.0, None, ALU.mult),
                reads=[bs], writes=[bd])
            op("vector", lambda e, wsrc=wsrc, wdst=wdst, n=n: e.tensor_copy(
                wdst[:, 1:n:2], wsrc[:, 0:n - 1:2]), reads=[bs], writes=[bd])

        bld.hard_sync = True
        arena_b.reset(to=KEEP_B)
        arena_f.reset(to=KEEP_F)
        wg_sb = arena_b.alloc(NCH * E)
        combt = arena_b.alloc(16 * 128)
        b2_sb = arena_b.alloc(1024)
        p3_x = [arena_f.alloc(1024) for _ in range(2)]
        bg_sb = arena_f.alloc(E)
        g_sc = arena_f.alloc(64)
        g_v1 = arena_f.alloc(E)
        g_v2 = arena_f.alloc(E)
        g_v3 = arena_f.alloc(E)

        b_xnown = bld.buf()
        b_wg, b_combt, b_b2 = bld.buf(), bld.buf(), bld.buf()
        b_bg = bld.buf()
        b_p3x = [bld.buf(), bld.buf()]
        b_comb = bld.buf()
        b_gsc, b_gv1, b_gv2, b_gv3 = bld.buf(), bld.buf(), bld.buf(), bld.buf()
        b_acc = Buf()

        op("gpsimd", lambda e: e.dma_start(out=wg_sb[:, 0:NCH * E], in_=wg_l),
           writes=[b_wg], inc=16)
        op("gpsimd", lambda e: e.dma_start(out=bg_sb[:, 0:E], in_=bg_bc_in),
           writes=[b_bg], inc=16)
        op("gpsimd", lambda e: e.dma_start(
            out=b2_sb.t[0:128, b2_sb.off:b2_sb.off + 1024], in_=b2_l),
           writes=[b_b2], inc=16)

        for c in range(NCH):
            for hh in range(2):
                op("sync", lambda e, c=c, hh=hh: e.dma_start_transpose(
                    xn_own[:, c * 2048 + hh * 1024:c * 2048 + (hh + 1) * 1024],
                    xn_ag_in.ap()[hh * 1024:(hh + 1) * 1024,
                                  128 * c:128 * (c + 1)]),
                   reads=[d_xnagin_h[hh]], writes=[b_xnown], inc=16)

        for i in range(16):
            gbank = 6
            fns = []
            for k in range(NCH):
                fns.append(lambda e, k=k, i=i: e.matmul(
                    ps[gbank][:, 0:E],
                    xn_own[:, k * 2048 + i * 128:k * 2048 + (i + 1) * 128],
                    wg_sb[:, k * E:(k + 1) * E],
                    start=(k == 0), stop=(k == NCH - 1)))
            group("tensor", fns, reads=[b_xnown, b_wg], writes=[psb[gbank]])
            lg = g_sc[:, 0:E]
            m1 = g_sc[:, 8:9]
            nm1 = g_sc[:, 12:13]
            m2 = g_sc[:, 16:17]
            dn = g_sc[:, 20:21]
            rd = g_sc[:, 24:25]
            op("vector", lambda e, lg=lg: e.tensor_tensor(
                lg, ps[6][:, 0:E], bg_sb[:, 0:E], ALU.add),
                reads=[psb[6], b_bg], writes=[b_gsc])
            op("vector", lambda e, m1=m1, lg=lg: e.reduce_max(m1, lg, axis=AX),
               reads=[b_gsc], writes=[b_gsc])
            op("vector", lambda e, nm1=nm1, m1=m1: e.tensor_scalar(
                nm1, m1, -1.0, None, ALU.mult), reads=[b_gsc], writes=[b_gsc])
            op("vector", lambda e, lg=lg, m1=m1: e.tensor_scalar(
                g_v1[:, 0:E], lg, m1, BIGNEG, ALU.is_equal, op1=ALU.mult),
                reads=[b_gsc], writes=[b_gv1])
            op("vector", lambda e, lg=lg: e.tensor_tensor(
                g_v2[:, 0:E], lg, g_v1[:, 0:E], ALU.add),
                reads=[b_gsc, b_gv1], writes=[b_gv2])
            op("vector", lambda e, m2=m2: e.reduce_max(
                m2, g_v2[:, 0:E], axis=AX), reads=[b_gv2], writes=[b_gsc])
            op("vector", lambda e, lg=lg, m2=m2: e.tensor_scalar(
                g_v1[:, 0:E], lg, m2, None, ALU.is_ge),
                reads=[b_gsc], writes=[b_gv1])
            op("scalar", lambda e, lg=lg, nm1=nm1: e.activation(
                out=g_v3[:, 0:E], in_=lg, func=AF.Exp, bias=nm1, scale=1.0),
                reads=[b_gsc], writes=[b_gv3])
            op("vector", lambda e: e.tensor_mul(
                g_v3[:, 0:E], g_v3[:, 0:E], g_v1[:, 0:E]),
                reads=[b_gv1, b_gv3], writes=[b_gv3])
            op("vector", lambda e, dn=dn: e.reduce_sum(
                dn, g_v3[:, 0:E], axis=AX), reads=[b_gv3], writes=[b_gsc])
            op("vector", lambda e, rd=rd, dn=dn: e.reciprocal(rd, dn),
               reads=[b_gsc], writes=[b_gsc])
            op("vector", lambda e, i=i, rd=rd: e.tensor_scalar(
                comb_f[:, i * E * 4:(i + 1) * E * 4:4], g_v3[:, 0:E], rd,
                None, ALU.mult),
                reads=[b_gv3, b_gsc], writes=[b_comb])
            op("tensor", lambda e, i=i: e.transpose(
                ps[7][0:E, 0:128],
                comb_f.t[:, comb_f.off + i * E * 4:
                         comb_f.off + (i + 1) * E * 4:4],
                ident[:, :]),
                reads=[b_comb, b_ident], writes=[psb[7]])
            op("vector", lambda e, i=i: e.tensor_copy(
                combt.t[0:E, combt.off + i * 128:combt.off + (i + 1) * 128],
                ps[7][0:E, 0:128]),
                reads=[psb[7]], writes=[b_combt])

        for i in range(16):
            sl = i % 2
            xt, bx = p3_x[sl], b_p3x[sl]
            op("sync", lambda e, i=i, xt=xt: e.dma_start(
                out=xt[:, 0:1024], in_=x_own[128 * i:128 * (i + 1), :]),
                writes=[bx], inc=16)
            for dh in range(2):
                bank = 2 + (i % 2)
                op("tensor", lambda e, i=i, dh=dh, bank=bank: e.matmul(
                    ps[bank][:, :],
                    combt.t[0:E, combt.off + i * 128:combt.off + (i + 1) * 128],
                    b2_sb.t[0:8, b2_sb.off + dh * 512:b2_sb.off + (dh + 1) * 512],
                    start=True, stop=True),
                    reads=[b_combt, b_b2], writes=[psb[bank]])
                op("vector", lambda e, i=i, dh=dh, bank=bank, xt=xt:
                   e.tensor_tensor(acc[:, i * 1024 + dh * 512:
                                       i * 1024 + (dh + 1) * 512],
                                   xt[:, dh * 512:(dh + 1) * 512],
                                   ps[bank][:, :], ALU.add),
                   reads=[bx, psb[bank]], writes=[b_acc])

        bld.hard_sync = False

        for hh in range(2):
            op("gpsimd", lambda e, hh=hh: e.collective_compute(
                "AllGather", ALU.bypass,
                replica_groups=[list(range(NCORES))],
                ins=[xn_ag_in.ap()[hh * 1024:(hh + 1) * 1024, :].opt()],
                outs=[xn_all_h[hh].ap().opt()]),
                reads=[d_xnagin_h[hh]], writes=[d_xnall_h[hh]])

        bld.new_phase()

        b_xnq = [bld.buf(), bld.buf()]
        b_k = bld.buf()
        b_q = bld.buf()
        b_v = bld.buf()
        b_probs = [bld.buf() for _ in range(4)]
        b_sqt = [bld.buf(), bld.buf()]
        b_cstage = bld.buf()
        b_t1, b_t2 = bld.buf(), bld.buf()
        b_ctxf = bld.buf()
        b_rz = bld.buf()
        b_negc = Buf()
        b_maxes = Buf()
        b_cstat = Buf()

        op("vector", lambda e: e.memset(ctx_f.t[64:128,
                                                ctx_f.off:ctx_f.off + 512], 0.0),
           writes=[b_ctxf])
        op("vector", lambda e: e.memset(
            v_tok.t[:, v_tok.off + 64:v_tok.off + 32 * 130:130], 1.0),
           writes=[b_v])
        op("vector", lambda e: e.memset(
            v_tok.t[:, v_tok.off + 129:v_tok.off + 32 * 130:130], 1.0),
           writes=[b_v])

        ag1_tok = [d_xnall_h[0].w, d_xnall_h[1].w]

        for s in range(SEQS):
            for ttg in range(8):
                sl = ttg % 2
                xq, bq = xnq[sl], b_xnq[sl]
                t0 = s * 4096 + ttg * 512
                t_core = t0 // 2048
                t_in = t0 % 2048
                hlf = t_in // 1024
                row = t_core * 1024 + (t_in % 1024)
                for c in range(NCH):
                    op("sync", lambda e, c=c, row=row, hlf=hlf, xq=xq:
                       e.dma_start_transpose(
                           xq[:, c * 512:(c + 1) * 512],
                           xn_all_h[hlf].ap()[row:row + 512,
                                              128 * c:128 * (c + 1)]),
                       writes=[bq], extra=[ag1_tok[hlf]], inc=16)
                if True:
                    pos = ttg * 512
                    xsl = (lambda k, a, b2, xq=xq:
                           xq[:, k * 512 + a:k * 512 + b2])
                    for (wt, bw, bank) in ((w_k, b_wk, 0), (w_kr, b_wkr, 1),
                                           (w_q, b_wq, 2), (w_qr, b_wqr, 3)):
                        fns = []
                        for k in range(NCH):
                            fns.append(lambda e, wt=wt, k=k, bank=bank, xsl=xsl:
                                       e.matmul(ps[bank][:, :],
                                                wt[:, k * 128:(k + 1) * 128],
                                                xsl(k, 0, 512),
                                                start=(k == 0), stop=(k == NCH - 1)))
                        group("tensor", fns, reads=[bw, bq], writes=[psb[bank]])
                    for (bank_a, bank_r, dst, bd) in ((0, 1, k_rope, b_k),
                                                      (2, 3, q_rope, b_q)):
                        op("vector", lambda e, bank_a=bank_a, pos=pos: e.tensor_mul(
                            t1_f[:, 0:512], ps[bank_a][:, :],
                            cos_b[:, pos:pos + 512]),
                            reads=[psb[bank_a], b_cos], writes=[b_t1])
                        op("vector", lambda e, bank_r=bank_r, pos=pos: e.tensor_mul(
                            t2_f[:, 0:512], ps[bank_r][:, :],
                            sin_b[:, pos:pos + 512]),
                            reads=[psb[bank_r], b_sin], writes=[b_t2])
                        op("vector", lambda e, dst=dst, pos=pos: e.tensor_add(
                            dst[:, pos:pos + 512], t1_f[:, 0:512], t2_f[:, 0:512]),
                            reads=[b_t1, b_t2], writes=[bd])
                    for sub in range(4):
                        bank = 4 + (sub % 2)
                        fns = []
                        for k in range(NCH):
                            fns.append(lambda e, k=k, sub=sub, bank=bank, xsl=xsl:
                                       e.matmul(ps[bank][:, 0:128],
                                                xsl(k, sub * 128, (sub + 1) * 128),
                                                w_v[:, k * 128:(k + 1) * 128],
                                                start=(k == 0), stop=(k == NCH - 1)))
                        group("tensor", fns, reads=[b_wv, bq], writes=[psb[bank]])
                        kt = (pos + sub * 128) // 128
                        op("vector", lambda e, kt=kt, bank=bank: e.tensor_copy(
                            v_tok[:, kt * 130:kt * 130 + 64], ps[bank][:, 0:64]),
                            reads=[psb[bank]], writes=[b_v])
                        op("vector", lambda e, kt=kt, bank=bank: e.tensor_copy(
                            v_tok[:, kt * 130 + 65:kt * 130 + 129],
                            ps[bank][:, 64:128]),
                            reads=[psb[bank]], writes=[b_v])

            op("vector", lambda e: e.memset(maxes[:, :], 0.0), writes=[b_maxes])
            for (src, bs, col) in ((q_rope, b_q, 0), (k_rope, b_k, 2)):
                for i in range(8):
                    sl = i % 2
                    sq, bsq = sqt_b[sl], b_sqt[sl]
                    op("scalar", lambda e, src=src, i=i, sq=sq: e.activation(
                        out=sq[:, 0:512], in_=src[:, 512 * i:512 * (i + 1)],
                        func=AF.Square, bias=0.0, scale=1.0),
                        reads=[bs], writes=[bsq])
                    for hh in range(2):
                        bank = 6
                        op("tensor", lambda e, hh=hh, sq=sq: e.matmul(
                            ps[6][0:1, 0:512],
                            oneshd[:, hh:hh + 1], sq[:, 0:512],
                            start=True, stop=True),
                            reads=[b_oneshd, bsq], writes=[psb[6]])
                        op("vector", lambda e: e.reduce_max(
                            cstat[:, 0:1], ps[6][0:1, 0:512], axis=AX),
                            reads=[psb[6]], writes=[b_cstat])
                        cc = col + hh
                        op("vector", lambda e, cc=cc: e.tensor_max(
                            maxes[:, cc:cc + 1], maxes[:, cc:cc + 1],
                            cstat[:, 0:1]),
                            reads=[b_cstat, b_maxes], writes=[b_maxes])
            op("vector", lambda e: e.tensor_max(
                cstat[:, 1:2], maxes[:, 0:1], maxes[:, 1:2]),
                reads=[b_maxes], writes=[b_cstat])
            op("vector", lambda e: e.tensor_max(
                cstat[:, 2:3], maxes[:, 2:3], maxes[:, 3:4]),
                reads=[b_maxes, b_cstat], writes=[b_cstat])
            op("vector", lambda e: e.tensor_add(
                cstat[:, 3:4], cstat[:, 1:2], cstat[:, 2:3]),
                reads=[b_cstat], writes=[b_cstat])
            op("vector", lambda e: e.tensor_scalar(
                cstat[:, 4:5], cstat[:, 3:4], -0.0635, -1.0, ALU.mult,
                op1=ALU.add), reads=[b_cstat], writes=[b_cstat])
            op("tensor", lambda e: e.matmul(
                ps[6][0:128, 0:1], onesrow[0:1, 0:128], cstat[0:1, 4:5],
                start=True, stop=True),
                reads=[b_onesrow, b_cstat], writes=[psb[6]])
            op("vector", lambda e: e.tensor_copy(negc_bc[:, 0:1], ps[6][:, 0:1]),
               reads=[psb[6]], writes=[b_negc])

            for qp in range(4):
                qts = [2 * qp, 2 * qp + 1]
                nkts = [4 * (qts[0] + 1), 4 * (qts[1] + 1)]

                def c_scores(ci, kt):
                    qt, nkt = qts[ci], nkts[ci]
                    sbank = [2 * ci, 2 * ci + 1]
                    for h in range(2):
                        op("tensor", lambda e, h=h, kt=kt, qt=qt, sbank=sbank:
                           e.matmul(ps[sbank[h]][:, :],
                                    k_rope[64 * h:64 * (h + 1),
                                           kt * 128:(kt + 1) * 128],
                                    q_rope[64 * h:64 * (h + 1),
                                           qt * 512:(qt + 1) * 512],
                                    start=True, stop=True,
                                    tile_position=(64 * h, 0)),
                           reads=[b_k, b_q], writes=[psb[sbank[h]]])
                    if kt >= nkt - 4:
                        d_idx = kt - (nkt - 4)
                        op("vector", lambda e, d_idx=d_idx, ci=ci:
                           e.tensor_tensor(pp[2 * ci][:, :],
                                           pp[2 * ci][:, :],
                                           mask_b[:, d_idx * 1024:
                                                  (d_idx + 1) * 1024],
                                           ALU.add),
                           reads=[b_mask, psb[sbank[0]], psb[sbank[1]]],
                           writes=[psb[sbank[0]], psb[sbank[1]]])
                    pslot = 2 * ci + (kt % 2)
                    op("scalar", lambda e, ci=ci, pslot=pslot:
                       e.activation(out=probs[pslot][:, 0:1024],
                                    in_=pp[2 * ci][:, :], func=AF.Exp,
                                    bias=negc_bc[:, 0:1], scale=0.125),
                       reads=[psb[sbank[0]], psb[sbank[1]], b_negc],
                       writes=[b_probs[pslot]])

                def c_av(ci, kt):
                    nkt = nkts[ci]
                    cbank = [4 + 2 * ci, 5 + 2 * ci]
                    pslot = 2 * ci + (kt % 2)
                    for h in range(2):
                        op("tensor", lambda e, h=h, kt=kt, pslot=pslot,
                           cbank=cbank, nkt=nkt:
                           e.matmul(ps[cbank[h]][0:65, :],
                                    v_tok[:, kt * 130 + 65 * h:
                                          kt * 130 + 65 * h + 65],
                                    probs[pslot][:, h * 512:(h + 1) * 512],
                                    start=(kt == 0), stop=(kt == nkt - 1)),
                           reads=[b_v, b_probs[pslot]],
                           writes=[psb[cbank[h]]], inc=1)

                for step in range(max(nkts) + 1):
                    for ci in range(2):
                        if step < nkts[ci]:
                            c_scores(ci, step)
                    for ci in range(2):
                        if 1 <= step <= nkts[ci]:
                            c_av(ci, step - 1)

                for ci in range(2):
                    qt = qts[ci]
                    cbank = [4 + 2 * ci, 5 + 2 * ci]
                    cst, bcst = ctx_stage, b_cstage
                    for h in range(2):
                        op("vector", lambda e, h=h, cbank=cbank: e.tensor_copy(
                            ctx_f.t[0:65, ctx_f.off:ctx_f.off + 512],
                            ps[cbank[h]][0:65, :]),
                            reads=[psb[cbank[h]]], writes=[b_ctxf])
                        tfns = []
                        for sub in range(4):
                            tfns.append(lambda e, sub=sub, h=h, cbank=cbank:
                                        e.transpose(
                                            ps[cbank[h]][:, sub * 128:
                                                         (sub + 1) * 128],
                                            ctx_f.t[:, ctx_f.off + sub * 128:
                                                    ctx_f.off + (sub + 1) * 128],
                                            ident[:, :]))
                        group("tensor", tfns, reads=[b_ctxf, b_ident],
                              writes=[psb[cbank[h]]])
                        for sub in range(4):
                            op("vector", lambda e, sub=sub, h=h, cbank=cbank:
                               e.reciprocal(rz_f[:, sub * 4:sub * 4 + 1],
                                            ps[cbank[h]][:, sub * 128 + 64:
                                                         sub * 128 + 65]),
                               reads=[psb[cbank[h]]], writes=[b_rz])
                            op("vector", lambda e, sub=sub, h=h, cbank=cbank,
                               cst=cst:
                               e.tensor_scalar(
                                   cst[:, sub * 128 + 64 * h:
                                       sub * 128 + 64 * h + 64],
                                   ps[cbank[h]][:, sub * 128:sub * 128 + 64],
                                   rz_f[:, sub * 4:sub * 4 + 1], None, ALU.mult),
                               reads=[psb[cbank[h]], b_rz], writes=[bcst])
                    t0 = s * 4096 + qt * 512
                    op("sync", lambda e, t0=t0, cst=cst: e.dma_start(
                        out=ctx_ag_in.ap()[t0:t0 + 512, :].rearrange(
                            "(sub p) c -> p sub c", p=128),
                        in_=cst[:, 0:512].rearrange("p (sub c) -> p sub c",
                                                    c=128)),
                        reads=[bcst], writes=[d_ctxagin], inc=16)

        bld.new_phase()
        bld.hard_sync = True
        bld.window = 8
        bld.hist = []
        arena_b.reset(to=KEEP_B)
        arena_f.reset(to=KEEP_F)
        w1_sb = arena_b.alloc(16 * 8 * 128)
        w2_sb = arena_b.alloc(16 * 1024)
        h_sb = arena_b.alloc(16 * 512)
        ysc = [arena_f.alloc(1024) for _ in range(2)]
        b1_sb = arena_f.alloc(E * 16 * 4)
        b_w1h = [bld.buf(), bld.buf()]
        b_w2 = bld.buf()
        b_h = bld.buf()
        b_b1 = bld.buf()
        b_ysc = [bld.buf(), bld.buf()]

        op("gpsimd", lambda e: e.dma_start(out=b1_sb[:, 0:E * 16 * 4], in_=b1_l),
           writes=[b_b1], inc=16)
        for hf in range(2):
            op("gpsimd", lambda e, hf=hf: e.dma_start(
                out=w1_sb[:, hf * 8192:(hf + 1) * 8192],
                in_=w1_l[0, :, hf * 8192:(hf + 1) * 8192]),
                writes=[b_w1h[hf]], inc=16)
        op("gpsimd", lambda e: e.dma_start(
            out=w2_sb[:, 0:16384], in_=w2_l[0, :, :]), writes=[b_w2], inc=16)

        for e_i in range(E):
            for tt in range(4):
                tt_fence = [(en, bld.S.count[en]) for en in ENGINES
                            if en != "gpsimd" and bld.S.count[en] > 0]
                for j in range(16):
                    bank = j % 2
                    fns = []
                    for k in range(NCH):
                        fns.append(lambda e, j=j, k=k, tt=tt, bank=bank:
                                   e.matmul(ps[bank][:, :],
                                            w1_sb[:, (j * 8 + k) * 128:
                                                  (j * 8 + k + 1) * 128],
                                            xn_own[:, k * 2048 + tt * 512:
                                                   k * 2048 + (tt + 1) * 512],
                                            start=(k == 0), stop=(k == NCH - 1)))
                    group("tensor", fns, reads=[b_w1h[j // 8], b_xnown],
                          writes=[psb[bank]],
                          extra=(tt_fence if j == 0 else ()))
                    op("scalar", lambda e, j=j, e_i=e_i, bank=bank: e.activation(
                        out=h_sb[:, j * 512:(j + 1) * 512], in_=ps[bank][:, :],
                        func=AF.Relu,
                        bias=b1_sb[:, (e_i * 16 + j) * 4:(e_i * 16 + j) * 4 + 1],
                        scale=1.0),
                        reads=[psb[bank], b_b1], writes=[b_h])
                for u in range(4):
                    i = tt * 4 + u
                    for dh in range(2):
                        bank = 2 + dh
                        fns = []
                        for j in range(16):
                            fns.append(lambda e, j=j, u=u, dh=dh, bank=bank:
                                       e.matmul(ps[bank][:, :],
                                                h_sb[:, j * 512 + u * 128:
                                                     j * 512 + (u + 1) * 128],
                                                w2_sb[:, j * 1024 + dh * 512:
                                                      j * 1024 + (dh + 1) * 512],
                                                start=(j == 0), stop=(j == 15)))
                        group("tensor", fns, reads=[b_h, b_w2],
                              writes=[psb[bank]])
                    ysl = u % 2
                    op("scalar", lambda e, i=i, e_i=e_i, ysl=ysl:
                       e.activation(out=ysc[ysl][:, 0:1024],
                                    in_=pp23[:, :], func=AF.Copy,
                                    bias=0.0,
                                    scale=comb_f[:, (i * E + e_i) * 4:
                                                 (i * E + e_i) * 4 + 1]),
                       reads=[psb[2], psb[3], b_comb], writes=[b_ysc[ysl]])
                    op("vector", lambda e, i=i, ysl=ysl:
                       e.tensor_add(
                           acc[:, i * 1024:(i + 1) * 1024],
                           acc[:, i * 1024:(i + 1) * 1024],
                           ysc[ysl][:, 0:1024]),
                       reads=[b_ysc[ysl], b_acc], writes=[b_acc])
            if e_i + 1 < E:
                bld.hard_sync = False
                for hf in range(2):
                    op("gpsimd", lambda e, e_i=e_i, hf=hf: e.dma_start(
                        out=w1_sb[:, hf * 8192:(hf + 1) * 8192],
                        in_=w1_l[e_i + 1, :, hf * 8192:(hf + 1) * 8192]),
                        writes=[b_w1h[hf]], inc=16)
                op("gpsimd", lambda e, e_i=e_i: e.dma_start(
                    out=w2_sb[:, 0:16384], in_=w2_l[e_i + 1, :, :]),
                    writes=[b_w2], inc=16)
                bld.hard_sync = True

        bld.hard_sync = False
        bld.window = 0
        op("gpsimd", lambda e: e.collective_compute(
            "AllGather", ALU.bypass, replica_groups=[list(range(NCORES))],
            ins=[ctx_ag_in.ap().opt()], outs=[ctx_all.ap().opt()]),
            reads=[d_ctxagin], writes=[d_ctxall])

        bld.new_phase()
        arena_b.reset()
        arena_f.reset()
        ctx_fm = arena_b.alloc(NCORES * 2048)
        wo_sb = arena_b.alloc(NCORES * 1024)
        p4_out = [arena_f.alloc(512) for _ in range(2)]
        b_ctxfm = bld.buf()
        b_wo = bld.buf()
        b_out = [bld.buf(), bld.buf()]

        op("gpsimd", lambda e: e.dma_start(out=wo_sb[:, 0:NCORES * 1024],
                                           in_=wo_l), writes=[b_wo], inc=16)

        def gather_fn(e):
            with e.register("goff") as goff:
                pid = e.partition_id()
                e.reg_mov(goff, pid)
                e.reg_mul(goff, goff, OWN * 128)
                src = bass.AP(ctx_all, goff,
                              [[NTOK * 128, NCORES], [128, OWN], [1, 128]])
                dst = bass.AP(ctx_mine, 0,
                              [[OWN * 128, NCORES], [128, OWN], [1, 128]])
                return e.dma_start(out=dst, in_=src)
        op("gpsimd", gather_fn, reads=[d_ctxall], writes=[d_ctxmine], inc=16)

        for r in range(NCORES):
            op("sync", lambda e, r=r: e.dma_start_transpose(
                ctx_fm[:, r * 2048:(r + 1) * 2048],
                ctx_mine.ap()[r * OWN:(r + 1) * OWN, :]),
                reads=[d_ctxmine], writes=[b_ctxfm], inc=16)

        for i in range(16):
            for dh in range(2):
                bank = (i * 2 + dh) % 4
                fns = []
                for r in range(NCORES):
                    fns.append(lambda e, r=r, i=i, dh=dh, bank=bank:
                               e.matmul(ps[bank][:, :],
                                        ctx_fm[:, r * 2048 + i * 128:
                                               r * 2048 + (i + 1) * 128],
                                        wo_sb[:, r * 1024 + dh * 512:
                                              r * 1024 + (dh + 1) * 512],
                                        start=(r == 0), stop=(r == NCORES - 1)))
                group("tensor", fns, reads=[b_ctxfm, b_wo], writes=[psb[bank]])
                sl = (i * 2 + dh) % 2
                ot, bo = p4_out[sl], b_out[sl]
                op("vector", lambda e, i=i, dh=dh, bank=bank, ot=ot:
                   e.tensor_add(ot[:, 0:512], ps[bank][:, :],
                                acc[:, i * 1024 + dh * 512:
                                    i * 1024 + (dh + 1) * 512]),
                   reads=[psb[bank], b_acc], writes=[bo])
                op("sync", lambda e, i=i, dh=dh, ot=ot: e.dma_start(
                    out=y_own[i * 128:(i + 1) * 128, dh * 512:(dh + 1) * 512],
                    in_=ot[:, 0:512]),
                    reads=[bo], writes=[d_yout], inc=16)

        block = ctx.enter_context(nc.Block())
        final_sync = bld.S.count["sync"]

        def make_body(eng_name):
            def body(eng):
                last_wait = {e: 0 for e in ENGINES}
                for wd, fn, inc in bld.S.events[eng_name]:
                    for we, wv in wd.items():
                        if wv > last_wait[we]:
                            eng.wait_ge(sem[we], wv)
                            last_wait[we] = wv
                    inst = fn(eng)
                    if inc and inst is not None:
                        inst.then_inc(sem[eng_name], inc)
                if eng_name == "sync":
                    eng.wait_ge(sem["sync"], final_sync)
            return body

        block.tensor(make_body("tensor"))
        block.vector(make_body("vector"))
        block.scalar(make_body("scalar"))
        block.gpsimd(make_body("gpsimd"))
        block.sync(make_body("sync"))

    return nc


def _host_inputs(L0_attn_x, Wq, Wk, Wv, Wo, Wg, bg, W1, b1, W2, b2):
    f32 = np.float32
    x = np.ascontiguousarray(np.asarray(L0_attn_x, dtype=f32)).reshape(NTOK, D)

    def pair_slice(W, c):
        sl = np.asarray(W[:, 128 * c:128 * (c + 1)], dtype=f32)
        return np.ascontiguousarray(sl.reshape(NCH, 128, 128).transpose(1, 0, 2)
                                    .reshape(128, NCH * 128))

    Wo = np.asarray(Wo, dtype=f32)
    wo_l = np.ascontiguousarray(Wo.reshape(NCORES, 128, 1024)
                                .transpose(1, 0, 2).reshape(128, NCORES * 1024))
    Wg = np.asarray(Wg, dtype=f32)
    wg_l = np.ascontiguousarray(Wg.reshape(NCH, 128, E).transpose(1, 0, 2)
                                .reshape(128, NCH * E))
    bg_bc = np.ascontiguousarray(np.broadcast_to(np.asarray(bg, dtype=f32),
                                                 (128, E)))
    W1 = np.asarray(W1, dtype=f32)
    w1_l = np.ascontiguousarray(W1.reshape(E, NCH, 128, 16, 128)
                                .transpose(0, 2, 3, 1, 4)
                                .reshape(E, 128, 16 * NCH * 128))
    b1 = np.asarray(b1, dtype=f32)
    _b1c = b1.reshape(E, 16, 128).transpose(2, 0, 1).reshape(128, E * 16)
    b1_l = np.zeros((128, E * 16 * 4), dtype=f32)
    b1_l[:, 0::4] = _b1c
    b1_l = np.ascontiguousarray(b1_l)
    W2 = np.asarray(W2, dtype=f32)
    w2_l = np.ascontiguousarray(W2.reshape(E, 16, 128, 1024)
                                .transpose(0, 2, 1, 3).reshape(E, 128, 16 * 1024))
    _b2 = np.asarray(b2, dtype=f32)
    b2_l = np.zeros((128, D), dtype=f32)
    b2_l[0:E] = _b2
    b2_l = np.ascontiguousarray(b2_l)

    pos = np.arange(S, dtype=np.float64)
    d_idx = np.arange(128) % 64
    inv_freq = 1.0 / (10000.0 ** ((2 * (d_idx // 2)).astype(np.float64) / DK))
    theta = inv_freq[:, None] * pos[None, :]
    cos_t = np.cos(theta).astype(f32)
    sin_t = np.sin(theta).astype(f32)

    masks = np.zeros((128, 4, 2, 512), dtype=f32)
    ii = np.arange(128)[:, None]
    qq = np.arange(512)[None, :]
    for di, dd in enumerate((0, 128, 256, 384)):
        m = np.where(dd + ii > qq, BIGNEG, 0.0)
        masks[:, di, 0, :] = m
        masks[:, di, 1, :] = m
    masks = np.ascontiguousarray(masks.reshape(128, 4 * 1024))

    ident = np.eye(128, dtype=f32)
    oneshd = np.zeros((128, 2), dtype=f32)
    oneshd[0:64, 0] = 1.0
    oneshd[64:128, 1] = 1.0
    onesrow = np.ones((128, 128), dtype=f32)

    Wq = np.asarray(Wq, dtype=f32)
    Wk = np.asarray(Wk, dtype=f32)
    Wv = np.asarray(Wv, dtype=f32)

    in_maps = []
    for c in range(NCORES):
        in_maps.append({
            "x_own": np.ascontiguousarray(x[OWN * c:OWN * (c + 1)]),
            "wq_p": pair_slice(Wq, c),
            "wk_p": pair_slice(Wk, c),
            "wv_p": pair_slice(Wv, c),
            "wo_l": wo_l,
            "wg_l": wg_l,
            "bg_bc": bg_bc,
            "w1_l": w1_l,
            "b1_l": b1_l,
            "w2_l": w2_l,
            "b2_l": b2_l,
            "cos_t": cos_t,
            "sin_t": sin_t,
            "masks": masks,
            "ident": ident,
            "oneshd": oneshd,
            "onesrow": onesrow,
        })
    return in_maps


_NC_CACHE = None


def kernel(L0_attn_x, L0_attn_Wq, L0_attn_Wk, L0_attn_Wv, L0_attn_Wo,
           L0_ffn_Wg, L0_ffn_bg, L0_ffn_W1, L0_ffn_b1, L0_ffn_W2, L0_ffn_b2):
    global _NC_CACHE
    in_maps = _host_inputs(L0_attn_x, L0_attn_Wq, L0_attn_Wk, L0_attn_Wv,
                           L0_attn_Wo, L0_ffn_Wg, L0_ffn_bg, L0_ffn_W1,
                           L0_ffn_b1, L0_ffn_W2, L0_ffn_b2)
    if _NC_CACHE is None:
        _NC_CACHE = build_program()
    res = run_bass_kernel_spmd(_NC_CACHE, in_maps, core_ids=list(range(NCORES)))
    out = np.concatenate([res.results[c]["y_own"] for c in range(NCORES)],
                         axis=0)
    return out.reshape(B, S, D).astype(np.float32)
